# version 02 (final)
# baseline (speedup 1.0000x reference)
import math
import numpy as np
from contextlib import ExitStack

import concourse.bass as bass
import concourse.mybir as mybir
from concourse.bass_utils import run_bass_kernel_spmd

F32 = mybir.dt.float32
BF16 = mybir.dt.bfloat16
AF = mybir.ActivationFunctionType
ALU = mybir.AluOpType
AX = mybir.AxisListType.X

B, S, D = 4, 4096, 2048
H, KVH, HD = 16, 8, 128
KVD = KVH * HD
R = 32
NT = S // 128
NOT = 16
NCL = 8
EXP = 8
HID = 1408
HC = HID // 128
DC = D // 128
RMS_EPS = 1.1920929e-07
LN_EPS = 1e-5
ISQ = 1.0 / math.sqrt(HD)
MASK_NEG = -1.0e30
BBOUND_COEF = math.sqrt(HD)
BBOUND_MARGIN = 60.0

DEBUG_DUMPS = ()
BUILD_PHASES = ("A", "B", "C", "D", "E")


def host_prep(inputs):
    f = np.float32
    asf = lambda a: np.ascontiguousarray(np.asarray(a), dtype=f)
    g = {}

    def wimg(W, halves, outd):
        hw = outd // halves
        a = W.T.reshape(DC, 128, outd)
        a = np.transpose(a, (1, 0, 2))
        return np.ascontiguousarray(
            np.stack([a[:, :, i * hw:(i + 1) * hw] for i in range(halves)], 0))

    g["wq_sb"] = wimg(asf(inputs["L0_attn_W_q"]), 2, D)
    g["wk_sb"] = wimg(asf(inputs["L0_attn_W_k"]), 2, KVD)
    g["wv_sb"] = wimg(asf(inputs["L0_attn_W_v"]), 2, KVD)
    g["wo_sb"] = wimg(asf(inputs["L0_attn_W_o"]), 1, D)[0]

    def aimg(A):
        a = A.T.reshape(DC, 128, A.shape[0])
        return np.ascontiguousarray(np.transpose(a, (1, 0, 2)))
    akv = np.concatenate([asf(inputs["L0_attn_A_k"]), asf(inputs["L0_attn_A_v"])], 0)
    g["akv_sb"] = aimg(akv)
    g["aq_sb"] = aimg(asf(inputs["L0_attn_A_q"]))
    g["ao_sb"] = aimg(asf(inputs["L0_attn_A_o"]))
    g["bq_sb"] = np.ascontiguousarray(asf(inputs["L0_attn_B_q"]).T)
    g["bkv_sb"] = np.ascontiguousarray(np.concatenate(
        [asf(inputs["L0_attn_B_k"]).T, asf(inputs["L0_attn_B_v"]).T], 0))
    g["bo_sb"] = np.ascontiguousarray(asf(inputs["L0_attn_B_o"]).T)

    gw = asf(inputs["L0_ffn_gate_w"])
    a = gw.T.reshape(DC, 128, EXP)
    g["gw_sb"] = np.ascontiguousarray(np.transpose(a, (1, 0, 2)))

    w1 = np.concatenate([asf(inputs["L0_ffn_w1"]), asf(inputs["L0_ffn_sw1"])[None]], 0)
    w3 = np.concatenate([asf(inputs["L0_ffn_w3"]), asf(inputs["L0_ffn_sw3"])[None]], 0)
    w2 = np.concatenate([asf(inputs["L0_ffn_w2"]), asf(inputs["L0_ffn_sw2"])[None]], 0)

    def w13img(w):
        a = np.transpose(w, (0, 2, 1)).reshape(EXP + 1, DC, 128, HC, 128)
        return np.ascontiguousarray(np.transpose(a, (0, 3, 2, 1, 4)))
    g["w1_sb"] = w13img(w1)
    g["w3_sb"] = w13img(w3)
    g["w2_sb"] = np.ascontiguousarray(
        np.transpose(w2, (0, 2, 1)).reshape(EXP + 1, HC, 128, D))

    sm = np.zeros(64, f)
    sm[0] = np.asarray(inputs["L0_attn_s_q"], f).reshape(-1)[0]
    sm[1] = np.asarray(inputs["L0_attn_s_k"], f).reshape(-1)[0]
    sm[2] = np.asarray(inputs["L0_attn_s_v"], f).reshape(-1)[0]
    sm[3] = np.asarray(inputs["L0_attn_s_o"], f).reshape(-1)[0]
    sm[8] = LN_EPS
    sm[16:32] = asf(inputs["L0_attn_q_gain"])
    sm[32:40] = asf(inputs["L0_ffn_expert_bias"])
    g["smalls"] = np.ascontiguousarray(np.broadcast_to(sm[None, :], (128, 64)))

    g["ident"] = np.eye(128, dtype=f)
    g["ones_c"] = np.ones((128, 128), f)
    sel = np.zeros((EXP, (EXP + 1) * 128), f)
    for e in range(EXP):
        sel[e, e * 128:(e + 1) * 128] = 1.0
    g["sel8"] = sel

    inv = (1.0 / (10000.0 ** (np.arange(0, 64, 2, dtype=f) / f(64.0)))).astype(f)
    pos = np.arange(S, dtype=f)
    fr = (pos[:, None] * inv[None, :]).astype(f)
    cosq = np.cos(fr.astype(np.float64)).astype(f)
    sinq = np.sin(fr.astype(np.float64)).astype(f)
    cossin = np.concatenate([np.tile(cosq, (1, 8)), np.tile(sinq, (1, 8))], 1)
    g["cossin"] = np.ascontiguousarray(cossin)

    xs_all = asf(inputs["L0_attn_x"])
    per_core = []
    for c in range(8):
        b, hh = c // 2, c % 2
        m = dict(g)
        m["xs"] = np.ascontiguousarray(xs_all[b])
        own = [2 * s + hh for s in range(NOT)]
        m["xq"] = np.ascontiguousarray(
            xs_all[b].reshape(NT, 128, D)[own].reshape(NOT * 128, D))
        m["cossin_q"] = np.ascontiguousarray(
            cossin.reshape(NT, 128, 512)[own].reshape(NOT * 128, 512))
        mq = np.zeros((128, 8, 512), f)
        kk = np.arange(128)[:, None]
        qq = np.arange(128)[None, :]
        for o in range(8):
            for i in range(4):
                v = o - (2 * i + hh)
                if v == 0:
                    mq[:, o, i * 128:(i + 1) * 128] = np.where(qq >= kk, 0.0, MASK_NEG)
                elif v > 0:
                    mq[:, o, i * 128:(i + 1) * 128] = MASK_NEG
        m["maskq"] = np.ascontiguousarray(mq)
        per_core.append(m)
    return per_core


class Ctr:
    __slots__ = ("h", "n", "name")
    def __init__(self, name):
        self.h = None; self.n = 0; self.name = name
    def inc(self, k=1):
        self.n += k
        return self.n


class Ctx:
    pass


def build_program(debug_dumps=DEBUG_DUMPS, phases=BUILD_PHASES):
    nc = bass.Bass("TRN2", target_bir_lowering=False)
    cx = Ctx()
    cx.nc = nc
    cx.debug = set(debug_dumps)
    cx.st = {"t": [], "v": [], "s": [], "g": [], "y": []}
    cx.dg = Ctr("dg"); cx.dy = Ctr("dy")
    cx.sT = Ctr("sT"); cx.sV = Ctr("sV"); cx.sS = Ctr("sS")
    cx.t_ = cx.st["t"].append
    cx.v_ = cx.st["v"].append
    cx.s_ = cx.st["s"].append
    cx.g_ = cx.st["g"].append
    cx.y_ = cx.st["y"].append

    d = {}
    cx.dram = d
    def din(name, shape, dtype=F32):
        d[name] = nc.dram_tensor(name, list(shape), dtype, kind="ExternalInput").ap()
    def dout(name, shape, dtype=F32):
        d[name] = nc.dram_tensor(name, list(shape), dtype, kind="ExternalOutput").ap()
    def dscr(name, shape, dtype):
        d[name] = nc.dram_tensor(name, list(shape), dtype, kind="Internal").ap()

    need_D = "D" in phases
    need_C = "C" in phases
    din("xs", [S, D]); din("xq", [NOT * 128, D]); din("cossin_q", [NOT * 128, 512])
    din("wq_sb", [2, 128, DC, 1024]); din("wk_sb", [2, 128, DC, 512]); din("wv_sb", [2, 128, DC, 512])
    din("akv_sb", [128, DC, 64]); din("aq_sb", [128, DC, R])
    din("bq_sb", [R, D]); din("bkv_sb", [2 * R, KVD])
    if need_C:
        din("wo_sb", [128, DC, D]); din("ao_sb", [128, DC, R]); din("bo_sb", [R, D])
        din("gw_sb", [128, DC, EXP])
    if need_D:
        din("w1_sb", [EXP + 1, HC, 128, DC, 128]); din("w3_sb", [EXP + 1, HC, 128, DC, 128])
        din("w2_sb", [EXP + 1, HC, 128, D])
    din("smalls", [128, 64]); din("ident", [128, 128]); din("ones_c", [128, 128])
    din("sel8", [EXP, (EXP + 1) * 128]); din("maskq", [128, 8, 512]); din("cossin", [S, 512])

    dscr("yt_s", [NOT, H, 128, 128], BF16)
    dscr("x1t_s", [128, DC, NOT * 128], BF16)
    dscr("x1_s", [NOT, 128, D], F32)
    dout("out", [NOT * 128, D])
    if "KT" in cx.debug or "KT2" in cx.debug:
        dout("dbg_KT", [2, 128, 4, S], BF16)
        dout("dbg_V", [2, 128, NT, 512], BF16)
    if "yT" in cx.debug:
        dout("dbg_yT", [NOT, H, 128, 128], BF16)
    if "qT" in cx.debug:
        dout("dbg_qT", [128, 8, 512], BF16)
        dout("dbg_kb", [128, 1024])
    if "PT" in cx.debug:
        dout("dbg_PT", [128, 4, 512], BF16)
        dout("dbg_pS", [128, 512])
    if "C30" in cx.debug:
        dout("dbg_sc30", [128, 512])
        dout("dbg_pt30", [128, 512], BF16)
        dout("dbg_kt30", [128, 128], BF16)
        dout("dbg_v30", [128, 512], BF16)
        dout("dbg_qt30", [128, 8, 512], BF16)
    if "pO" in cx.debug:
        dout("dbg_pO", [128, 1024])
    if "PT" in cx.debug:
        dout("dbg_mq", [128, 8, 512], BF16)
    if "x1" in cx.debug:
        dout("dbg_x1", [NOT, 128, D], F32)
        dout("dbg_x1t", [128, DC, NOT * 128], BF16)
        dout("dbg_gsel", [128, NOT, EXP], F32)
    if "ACC" in cx.debug:
        dout("dbg_ACC", [128, 8, D])

    es = ExitStack()
    E = es.enter_context
    for ctr in (cx.dg, cx.dy, cx.sT, cx.sV, cx.sS):
        ctr.h = E(nc.semaphore(ctr.name))
    cx.pAll = E(nc.psum_tensor("pAll", [128, 4096], F32))
    cx.p = [cx.pAll.ap()[:, i * 512:(i + 1) * 512] for i in range(8)]

    cx.ident = E(nc.sbuf_tensor("identt", [128, 128], BF16))
    cx.ones = E(nc.sbuf_tensor("onest", [128, 128], BF16))
    cx.sel8 = E(nc.sbuf_tensor("sel8t", [EXP, (EXP + 1) * 128], BF16))
    cx.smalls = E(nc.sbuf_tensor("smallst", [128, 64], F32))
    cx.gainc = E(nc.sbuf_tensor("gainct", [128, H], F32))
    cx.negB = E(nc.sbuf_tensor("negBt", [128, H], F32))
    cx.stat = E(nc.sbuf_tensor("statt", [128, 64], F32))
    cx.gsel = E(nc.sbuf_tensor("gselt", [128, NOT, EXP], F32))
    cx.gselb = E(nc.sbuf_tensor("gselbt", [128, NOT, EXP], BF16))

    emit_preamble(cx)
    if "A" in phases:
        emit_AB(cx, "B" in phases)
    if "C" in phases:
        emit_C(cx, es)
    if "D" in phases:
        emit_DE(cx, es, "E" in phases)

    emit_tail(cx)

    with nc.Block() as block:
        @block.gpsimd
        def _(g):
            for fn in cx.st["g"]: fn(g)
        @block.sync
        def _(y):
            for fn in cx.st["y"]: fn(y)
        @block.tensor
        def _(t):
            for fn in cx.st["t"]: fn(t)
        @block.vector
        def _(v):
            for fn in cx.st["v"]: fn(v)
        @block.scalar
        def _(s):
            for fn in cx.st["s"]: fn(s)
    es.close()
    return nc


def emit_preamble(cx):
    d = cx.dram
    for dst, src in ((cx.ident, d["ident"]), (cx.ones, d["ones_c"]), (cx.sel8, d["sel8"])):
        def f(g, dst=dst, src=src):
            g.dma_start(dst[:], src).then_inc(cx.dg.h, 16)
        cx.g_(f); cx.dg.inc(16)
    def f(y):
        y.dma_start(cx.smalls[:], d["smalls"]).then_inc(cx.dy.h, 16)
    cx.y_(f); cx.dy.inc(16)
    wsm = cx.dy.n
    s0 = cx.sS.n
    def f(s, wsm=wsm):
        s.wait_ge(cx.dy.h, wsm)
        s.activation(cx.negB[:], cx.smalls[:, 16:32], AF.Abs).then_inc(cx.sS.h, 1)
    cx.s_(f); cx.sS.inc(1)
    abs_done = cx.sS.n
    v0 = cx.sV.n
    def f(v, wsm=wsm, v0=v0, ws=abs_done):
        v.wait_ge(cx.dy.h, wsm)
        v.tensor_scalar(cx.gainc[:], cx.smalls[:, 16:32], float(ISQ), None, ALU.mult).then_inc(cx.sV.h, 1)
        v.wait_ge(cx.sS.h, ws)
        v.tensor_scalar(cx.negB[:], cx.negB[:], float(BBOUND_COEF), None, ALU.mult).then_inc(cx.sV.h, 1)
        v.wait_ge(cx.sV.h, v0 + 2)
        v.tensor_scalar(cx.negB[:], cx.negB[:], float(-BBOUND_MARGIN), None, ALU.add).then_inc(cx.sV.h, 1)
        v.wait_ge(cx.sV.h, v0 + 3)
        v.tensor_scalar(cx.negB[:], cx.negB[:], 0.0, None, ALU.max).then_inc(cx.sV.h, 1)
        v.wait_ge(cx.sV.h, v0 + 4)
        v.tensor_scalar(cx.negB[:], cx.negB[:], -1.0, None, ALU.mult).then_inc(cx.sV.h, 1)
    cx.v_(f); cx.sV.inc(5)
    cx.negB_ready = cx.sV.n
    cx.consts_g = cx.dg.n


def emit_AB(cx, do_B):
    nc = cx.nc
    d = cx.dram
    with ExitStack() as es:
        E = es.enter_context
        KT = E(nc.sbuf_tensor("KT", [128, 4, S], BF16))
        Vr = E(nc.sbuf_tensor("Vr", [128, NT, 512], BF16))
        WkT = E(nc.sbuf_tensor("WkT", [128, DC, 512], BF16))
        WvT = E(nc.sbuf_tensor("WvT", [128, DC, 512], BF16))
        WqT = E(nc.sbuf_tensor("WqT", [128, DC, 1024], BF16))
        Akv = E(nc.sbuf_tensor("Akv", [128, DC, 64], BF16))
        Aq = E(nc.sbuf_tensor("Aq", [128, DC, R], BF16))
        Bkv = E(nc.sbuf_tensor("Bkv", [2 * R, 512], BF16))
        Bq = E(nc.sbuf_tensor("Bq", [R, 1024], BF16))
        laev = E(nc.sbuf_tensor("laev", [64, 128], BF16))
        kq = E(nc.sbuf_tensor("kq", [128, 1024], F32))
        kb = E(nc.sbuf_tensor("kb", [128, 1024], BF16))
        xbf = E(nc.sbuf_tensor("xbf", [128, 2, D], BF16))
        xT = E(nc.sbuf_tensor("xT", [128, 2, DC, 128], BF16))
        cosst = E(nc.sbuf_tensor("cosst", [128, 2, 512], F32))
        maskq = E(nc.sbuf_tensor("maskqt", [128, 8, 512], BF16))
        qT = E(nc.sbuf_tensor("qT", [128, 8, 512], BF16))
        PT = E(nc.sbuf_tensor("PT", [128, 4, 512], BF16))
        mskd = E(nc.sbuf_tensor("mskd", [128, 2, 512], F32))
        ropes = E(nc.sbuf_tensor("ropes", [128, 256], F32))
        yTg = E(nc.sbuf_tensor("yTg", [128, 8, 512], BF16))
        rb = E(nc.sbuf_tensor("rbt", [128, 512], F32))
        rowi = E(nc.sbuf_tensor("rowi", [1, 512], F32))
        rowb = E(nc.sbuf_tensor("rowb", [1, 512], BF16))
        sqt = E(nc.sbuf_tensor("sqt", [128, 512], BF16))

        for dst, src2 in ((maskq, d["maskq"]), (Akv, d["akv_sb"]), (Aq, d["aq_sb"])):
            def f(g, dst=dst, src2=src2):
                g.dma_start(dst[:], src2).then_inc(cx.dg.h, 16)
            cx.g_(f); cx.dg.inc(16)

        st = Ctx()
        st.pe_weights_free = 0
        st.b_done_sV = 0
        st.b_done_sS = 0
        L = locals()
        for hh in (0, 1):
            emit_A_half(cx, hh, L, st)
            if do_B:
                emit_B_half(cx, hh, L, st)
    cx.AB_done = dict(sT=cx.sT.n, sV=cx.sV.n, sS=cx.sS.n, dy=cx.dy.n, dg=cx.dg.n)


def _emit_rope(cx, v, psrc_list, cosap, kq, ropes, nheads_per_p, v0):
    n = 0
    base = 0
    for pi, pp in enumerate(psrc_list):
        nh = nheads_per_p
        k1 = pp
        kqv = kq.ap()[:, pi * nh * 128:(pi + 1) * nh * 128].rearrange("q (h n) -> q h n", h=nh)
        rs = ropes.ap().rearrange("q (h n) -> q h n", h=8)[:, 0:nh, :]
        v.tensor_tensor(kqv[:, :, 0:32], k1[:, :, 0:32], cosap[:, 0:nh, :], ALU.mult).then_inc(cx.sV.h, 1); n += 1
        v.tensor_tensor(rs, k1[:, :, 32:64], cosap[:, 8:8 + nh, :], ALU.mult).then_inc(cx.sV.h, 1); n += 1
        v.wait_ge(cx.sV.h, v0 + n)
        v.tensor_tensor(kqv[:, :, 0:32], kqv[:, :, 0:32], rs, ALU.add).then_inc(cx.sV.h, 1); n += 1
        v.tensor_tensor(kqv[:, :, 32:64], k1[:, :, 32:64], cosap[:, 0:nh, :], ALU.mult).then_inc(cx.sV.h, 1); n += 1
        v.tensor_tensor(rs, k1[:, :, 0:32], cosap[:, 8:8 + nh, :], ALU.mult).then_inc(cx.sV.h, 1); n += 1
        v.wait_ge(cx.sV.h, v0 + n)
        v.tensor_tensor(kqv[:, :, 32:64], kqv[:, :, 32:64], rs, ALU.subtract).then_inc(cx.sV.h, 1); n += 1
        v.tensor_copy(kqv[:, :, 64:128], k1[:, :, 64:128]).then_inc(cx.sV.h, 1); n += 1
    return n


def emit_A_half(cx, hh, L, st):
    d = cx.dram
    p = cx.p
    KT, Vr, WkT, WvT = L["KT"], L["Vr"], L["WkT"], L["WvT"]
    Akv, Bkv, laev = L["Akv"], L["Bkv"], L["laev"]
    kq, kb, xbf, xT = L["kq"], L["kb"], L["xbf"], L["xT"]
    cosst, sqt, ropes = L["cosst"], L["sqt"], L["ropes"]
    pK, pV, pX, pL, pT2 = p[0], p[1], p[2], p[3], p[4]

    wpe = st.pe_weights_free
    for dst, src in ((WkT, d["wk_sb"][hh]), (WvT, d["wv_sb"][hh]),
                     (Bkv, d["bkv_sb"][:, hh * 512:(hh + 1) * 512])):
        def f(g, dst=dst, src=src, wpe=wpe):
            g.wait_ge(cx.sT.h, wpe)
            g.dma_start(dst[:], src).then_inc(cx.dg.h, 16)
        cx.g_(f); cx.dg.inc(16)
    wld = cx.dg.n
    def f(v, wld=wld):
        v.wait_ge(cx.dg.h, wld)
        v.tensor_scalar(WkT.ap().rearrange("p c n -> p (c n)"),
                        WkT.ap().rearrange("p c n -> p (c n)"),
                        cx.smalls[:, 1:2], None, ALU.mult).then_inc(cx.sV.h, 1)
        v.tensor_scalar(WvT.ap().rearrange("p c n -> p (c n)"),
                        WvT.ap().rearrange("p c n -> p (c n)"),
                        cx.smalls[:, 2:3], None, ALU.mult).then_inc(cx.sV.h, 1)
    cx.v_(f)
    w_scaled = cx.sV.inc(2)

    m_xload = [0] * (NT + 1)
    m_cload = [0] * (NT + 1)
    m_xpose = [0] * NT
    m_evx = [[0] * 4 for _ in range(NT)]
    m_lora = [0] * NT
    m_laev = [0] * NT
    m_proj = [0] * NT
    m_sq = [0] * NT
    m_ms = [0] * NT
    m_sqrt = [0] * NT
    m_kb = [0] * NT
    m_vev = [0] * NT
    m_ktT = [0] * NT
    m_ktev = [0] * NT

    def issue_loads(tg):
        slot = tg % 2
        wT = m_xpose[tg - 2] if tg >= 2 else getattr(st, "b_xpose_last", 0)
        wV = m_kb[tg - 2] if tg >= 2 else getattr(st, "b_kb_last", 0)
        def f(g, tg=tg, slot=slot, wT=wT):
            if wT:
                g.wait_ge(cx.sT.h, wT)
            g.dma_start(xbf[:, slot, :], d["xs"][tg * 128:(tg + 1) * 128, :]).then_inc(cx.dg.h, 16)
        cx.g_(f)
        m_xload[tg] = cx.dg.inc(16)
        def f(y, tg=tg, slot=slot, wV=wV):
            if wV:
                y.wait_ge(cx.sV.h, wV)
            y.dma_start(cosst[:, slot, :], d["cossin"][tg * 128:(tg + 1) * 128, :]).then_inc(cx.dy.h, 16)
        cx.y_(f)
        m_cload[tg] = cx.dy.inc(16)

    issue_loads(0)
    for tg in range(NT):
        slot = tg % 2
        if tg + 1 < NT:
            issue_loads(tg + 1)

        for b4 in range(4):
            if b4 == 0:
                wv = m_evx[tg - 1][3] if tg > 0 else max(st.b_done_sV, 0)
            else:
                wv = m_evx[tg][b4 - 1]
            def f(t, b4=b4, slot=slot, wx=m_xload[tg], wv=wv):
                if b4 == 0:
                    t.wait_ge(cx.dg.h, wx)
                if wv:
                    t.wait_ge(cx.sV.h, wv)
                for j in range(4):
                    c = b4 * 4 + j
                    ins = t.matmul(pX[:, j * 128:(j + 1) * 128],
                                   xbf[:, slot, c * 128:(c + 1) * 128], cx.ident[:],
                                   start=True, stop=True)
                ins.then_inc(cx.sT.h, 1)
            cx.t_(f)
            mt = cx.sT.inc(1)
            if b4 == 3:
                m_xpose[tg] = mt
            def f(v, b4=b4, slot=slot, mt=mt):
                v.wait_ge(cx.sT.h, mt)
                v.tensor_copy(xT.ap()[:, slot, b4 * 4:(b4 + 1) * 4, :],
                              pX.rearrange("q (c n) -> q c n", c=4)).then_inc(cx.sV.h, 1)
            cx.v_(f)
            m_evx[tg][b4] = cx.sV.inc(1)

        def f(t, slot=slot, wv=m_evx[tg][3], wk=w_scaled,
              wfs=(m_sq[tg - 1] if tg else st.b_done_sS),
              wfv=(max(m_kb[tg - 1], m_vev[tg - 1]) if tg else 0)):
            t.wait_ge(cx.sV.h, wv)
            t.wait_ge(cx.sV.h, wk)
            if wfs:
                t.wait_ge(cx.sS.h, wfs)
            if wfv:
                t.wait_ge(cx.sV.h, wfv)
            for dc in range(DC):
                t.matmul(pK[:, 0:512], xT[:, slot, dc, :], WkT[:, dc, :],
                         start=(dc == 0), stop=False)
                t.matmul(pV[:, 0:512], xT[:, slot, dc, :], WvT[:, dc, :],
                         start=(dc == 0), stop=False)
            for dc in range(DC):
                ins = t.matmul(pL[0:64, 0:128], Akv[:, dc, :], xT[:, slot, dc, :],
                               start=(dc == 0), stop=(dc == DC - 1))
            ins.then_inc(cx.sT.h, 1)
        cx.t_(f)
        m_lora[tg] = cx.sT.inc(1)

        def f(v, mt=m_lora[tg]):
            v.wait_ge(cx.sT.h, mt)
            v.tensor_copy(laev[:, :], pL[0:64, 0:128]).then_inc(cx.sV.h, 1)
        cx.v_(f)
        m_laev[tg] = cx.sV.inc(1)

        def f(t, wv=m_laev[tg]):
            t.wait_ge(cx.sV.h, wv)
            t.matmul(pK[:, 0:512], laev[0:32, :], Bkv[0:32, :], start=False, stop=True)
            t.matmul(pV[:, 0:512], laev[32:64, :], Bkv[32:64, :], start=False, stop=True).then_inc(cx.sT.h, 1)
        cx.t_(f)
        m_proj[tg] = cx.sT.inc(1)

        def f(s, mt=m_proj[tg], wv=(m_ms[tg - 1] if tg else 0)):
            s.wait_ge(cx.sT.h, mt)
            if wv:
                s.wait_ge(cx.sV.h, wv)
            for h4 in range(4):
                ins = s.activation(sqt[:, h4 * 128:(h4 + 1) * 128], pK[:, h4 * 128:(h4 + 1) * 128],
                                   AF.Square, accum_out=cx.stat[:, h4:h4 + 1])
            ins.then_inc(cx.sS.h, 1)
        cx.s_(f)
        m_sq[tg] = cx.sS.inc(1)

        def f(v, ws=m_sq[tg], wkbprev=(m_kb[tg - 1] if tg else 0)):
            v.wait_ge(cx.sS.h, ws)
            if wkbprev:
                v.wait_ge(cx.sV.h, wkbprev)
            v.tensor_scalar(cx.stat[:, 8:12], cx.stat[:, 0:4], 1.0 / HD, RMS_EPS,
                            ALU.mult, ALU.add).then_inc(cx.sV.h, 1)
        cx.v_(f)
        m_ms[tg] = cx.sV.inc(1)
        def f(s, wv=m_ms[tg]):
            s.wait_ge(cx.sV.h, wv)
            s.activation(cx.stat[:, 16:20], cx.stat[:, 8:12], AF.Sqrt).then_inc(cx.sS.h, 1)
        cx.s_(f)
        m_sqrt[tg] = cx.sS.inc(1)

        v0 = cx.sV.n
        def f(v, tg=tg, slot=slot, ws=m_sqrt[tg], v0=v0, wc=m_cload[tg],
              wexpprev=st.b_done_sS, wkt=(m_ktT[tg - 1] if tg else 0)):
            v.wait_ge(cx.sS.h, ws)
            v.wait_ge(cx.dy.h, wc)
            if wexpprev:
                v.wait_ge(cx.sS.h, wexpprev)
            if wkt:
                v.wait_ge(cx.sT.h, wkt)
            v.reciprocal(cx.stat[:, 24:28], cx.stat[:, 16:20]).then_inc(cx.sV.h, 1)
            co = cosst.ap()[:, slot, 0:512].rearrange("q (h n) -> q h n", h=16)
            k1 = pK.rearrange("q (h n) -> q h n", h=4)
            kqv = kq.ap()[:, 0:512].rearrange("q (h n) -> q h n", h=4)
            rs = ropes.ap().rearrange("q (h n) -> q h n", h=8)[:, 0:4, :]
            v.tensor_tensor(kqv[:, :, 0:32], k1[:, :, 0:32], co[:, 0:4, :], ALU.mult).then_inc(cx.sV.h, 1)
            v.tensor_tensor(rs, k1[:, :, 32:64], co[:, 8:12, :], ALU.mult).then_inc(cx.sV.h, 1)
            v.wait_ge(cx.sV.h, v0 + 3)
            v.tensor_tensor(kqv[:, :, 0:32], kqv[:, :, 0:32], rs, ALU.add).then_inc(cx.sV.h, 1)
            v.tensor_tensor(kqv[:, :, 32:64], k1[:, :, 32:64], co[:, 0:4, :], ALU.mult).then_inc(cx.sV.h, 1)
            v.tensor_tensor(rs, k1[:, :, 0:32], co[:, 8:12, :], ALU.mult).then_inc(cx.sV.h, 1)
            v.wait_ge(cx.sV.h, v0 + 6)
            v.tensor_tensor(kqv[:, :, 32:64], kqv[:, :, 32:64], rs, ALU.subtract).then_inc(cx.sV.h, 1)
            v.tensor_copy(kqv[:, :, 64:128], k1[:, :, 64:128]).then_inc(cx.sV.h, 1)
            v.wait_ge(cx.sV.h, v0 + 8)
            for h4 in range(4):
                v.tensor_scalar(kb[:, h4 * 128:(h4 + 1) * 128], kq[:, h4 * 128:(h4 + 1) * 128],
                                cx.stat[:, 24 + h4:25 + h4], None, ALU.mult).then_inc(cx.sV.h, 1)
        cx.v_(f)
        cx.sV.inc(12)
        m_kb[tg] = cx.sV.n

        def f(v, tg=tg, mt=m_proj[tg]):
            v.wait_ge(cx.sT.h, mt)
            v.tensor_copy(Vr[:, tg, :], pV[:, 0:512]).then_inc(cx.sV.h, 1)
        cx.v_(f)
        m_vev[tg] = cx.sV.inc(1)

        def f(t, wv=m_kb[tg], wktev=(m_ktev[tg - 1] if tg else 0)):
            t.wait_ge(cx.sV.h, wv)
            if wktev:
                t.wait_ge(cx.sV.h, wktev)
            for h4 in range(4):
                ins = t.matmul(pT2[:, h4 * 128:(h4 + 1) * 128], kb[:, h4 * 128:(h4 + 1) * 128],
                               cx.ident[:], start=True, stop=True)
            ins.then_inc(cx.sT.h, 1)
        cx.t_(f)
        m_ktT[tg] = cx.sT.inc(1)
        def f(v, tg=tg, mt=m_ktT[tg]):
            v.wait_ge(cx.sT.h, mt)
            v.tensor_copy(KT.ap()[:, :, tg * 128:(tg + 1) * 128],
                          pT2.rearrange("q (h n) -> q h n", h=4)).then_inc(cx.sV.h, 1)
        cx.v_(f)
        m_ktev[tg] = cx.sV.inc(1)

    st.a_done_sV = cx.sV.n
    st.a_done_sT = cx.sT.n
    st.a_xpose_last = m_xpose[NT - 1]
    st.a_kb_last = m_kb[NT - 1]

    if "KT" in cx.debug:
        def f(y, hh=hh, wv=cx.sV.n):
            y.wait_ge(cx.sV.h, wv)
            y.dma_start(d["dbg_KT"][hh], KT[:]).then_inc(cx.dy.h, 16)
            y.dma_start(d["dbg_V"][hh], Vr[:]).then_inc(cx.dy.h, 16)
        cx.y_(f)
        st.dbg_kt_dma = cx.dy.inc(32)


def emit_B_half(cx, hh, L, st):
    d = cx.dram
    p = cx.p
    KT, Vr, WqT, Aq, Bq = L["KT"], L["Vr"], L["WqT"], L["Aq"], L["Bq"]
    laev, kq, kb = L["laev"], L["kq"], L["kb"]
    xbf, xT, cosst, sqt, ropes = L["xbf"], L["xT"], L["cosst"], L["sqt"], L["ropes"]
    maskq, qT, PT, mskd, yTg, rb = L["maskq"], L["qT"], L["PT"], L["mskd"], L["yTg"], L["rb"]
    rowi, rowb = L["rowi"], L["rowb"]
    pSS = [p[0], p[1]]
    pO, pR, pQa, pQb, pX, pL = p[2], p[3], p[4], p[5], p[6], p[7]
    pT2 = p[6]

    def f(g, hh=hh, wpe=st.pe_weights_free):
        if wpe:
            g.wait_ge(cx.sT.h, wpe)
        g.dma_start(WqT[:], d["wq_sb"][hh]).then_inc(cx.dg.h, 16)
        g.dma_start(Bq[:], d["bq_sb"][:, hh * 1024:(hh + 1) * 1024]).then_inc(cx.dg.h, 16)
    cx.g_(f)
    wld = cx.dg.inc(32)
    def f(v, wld=wld):
        v.wait_ge(cx.dg.h, wld)
        v.tensor_scalar(WqT.ap().rearrange("p c n -> p (c n)"),
                        WqT.ap().rearrange("p c n -> p (c n)"),
                        cx.smalls[:, 0:1], None, ALU.mult).then_inc(cx.sV.h, 1)
    cx.v_(f)
    wq_scaled = cx.sV.inc(1)

    prev_p6_v = 0
    prev_pQ_free_v = 0
    prev_pQ_free_s = 0
    prev_headv = 0
    prev_yT_dma = 0
    last_exp_masked = [st.b_done_sS, st.b_done_sS]
    m_xpose = {}

    for j in range(4):
        s0 = 4 * j
        m_kbq = [0, 0, 0, 0]
        m_sqB = [0, 0, 0, 0]
        m_qTev = 0
        for i in range(4):
            sl = s0 + i
            slot = sl % 2
            wT = m_xpose.get(sl - 2, st.a_xpose_last if sl < 2 else 0)
            def f(g, sl=sl, slot=slot, wT=wT):
                if wT:
                    g.wait_ge(cx.sT.h, wT)
                g.dma_start(xbf[:, slot, :], d["xq"][sl * 128:(sl + 1) * 128, :]).then_inc(cx.dg.h, 16)
            cx.g_(f)
            m_x = cx.dg.inc(16)
            if i >= 2:
                wV = m_kbq[i - 2]
            elif j == 0:
                wV = max(prev_pQ_free_v, getattr(st, "a_kb_last", 0))
            else:
                wV = prev_pQ_free_v
            def f(y, sl=sl, slot=slot, wV=wV):
                if wV:
                    y.wait_ge(cx.sV.h, wV)
                y.dma_start(cosst[:, slot, :], d["cossin_q"][sl * 128:(sl + 1) * 128, :]).then_inc(cx.dy.h, 16)
            cx.y_(f)
            m_c = cx.dy.inc(16)

            m_evx = [0] * 4
            for b4 in range(4):
                wv = prev_p6_v if b4 == 0 else m_evx[b4 - 1]
                def f(t, b4=b4, slot=slot, wx=m_x, wv=wv):
                    if b4 == 0:
                        t.wait_ge(cx.dg.h, wx)
                    if wv:
                        t.wait_ge(cx.sV.h, wv)
                    for jj in range(4):
                        c = b4 * 4 + jj
                        ins = t.matmul(pX[:, jj * 128:(jj + 1) * 128],
                                       xbf[:, slot, c * 128:(c + 1) * 128], cx.ident[:],
                                       start=True, stop=True)
                    ins.then_inc(cx.sT.h, 1)
                cx.t_(f)
                mt = cx.sT.inc(1)
                if b4 == 3:
                    m_xpose[sl] = mt
                def f(v, b4=b4, slot=slot, mt=mt):
                    v.wait_ge(cx.sT.h, mt)
                    v.tensor_copy(xT.ap()[:, slot, b4 * 4:(b4 + 1) * 4, :],
                                  pX.rearrange("q (c n) -> q c n", c=4)).then_inc(cx.sV.h, 1)
                cx.v_(f)
                m_evx[b4] = cx.sV.inc(1)
            prev_p6_v = m_evx[3]

            wpv = m_kbq[i - 1] if i >= 1 else prev_pQ_free_v
            wps = m_sqB[i - 1] if i >= 1 else prev_pQ_free_s
            def f(t, slot=slot, wv=m_evx[3], wq=wq_scaled, wpv=wpv, wps=wps):
                t.wait_ge(cx.sV.h, wv)
                t.wait_ge(cx.sV.h, wq)
                if wpv:
                    t.wait_ge(cx.sV.h, wpv)
                if wps:
                    t.wait_ge(cx.sS.h, wps)
                for dc in range(DC):
                    t.matmul(pQa[:, 0:512], xT[:, slot, dc, :], WqT[:, dc, 0:512],
                             start=(dc == 0), stop=False)
                    t.matmul(pQb[:, 0:512], xT[:, slot, dc, :], WqT[:, dc, 512:1024],
                             start=(dc == 0), stop=False)
                for dc in range(DC):
                    ins = t.matmul(pL[0:32, 0:128], Aq[:, dc, :], xT[:, slot, dc, :],
                                   start=(dc == 0), stop=(dc == DC - 1))
                ins.then_inc(cx.sT.h, 1)
            cx.t_(f)
            m_l = cx.sT.inc(1)
            def f(v, mt=m_l):
                v.wait_ge(cx.sT.h, mt)
                v.tensor_copy(laev[0:32, :], pL[0:32, 0:128]).then_inc(cx.sV.h, 1)
            cx.v_(f)
            m_le = cx.sV.inc(1)
            def f(t, wv=m_le):
                t.wait_ge(cx.sV.h, wv)
                t.matmul(pQa[:, 0:512], laev[0:32, :], Bq[:, 0:512], start=False, stop=True)
                t.matmul(pQb[:, 0:512], laev[0:32, :], Bq[:, 512:1024], start=False, stop=True).then_inc(cx.sT.h, 1)
            cx.t_(f)
            m_qproj = cx.sT.inc(1)

            def f(s, mt=m_qproj, wv=wpv):
                s.wait_ge(cx.sT.h, mt)
                if wv:
                    s.wait_ge(cx.sV.h, wv)
                for h8 in range(8):
                    pp = pQa if h8 < 4 else pQb
                    o = (h8 % 4) * 128
                    ins = s.activation(sqt[:, o:o + 128], pp[:, o:o + 128],
                                       AF.Square, accum_out=cx.stat[:, h8:h8 + 1])
                ins.then_inc(cx.sS.h, 1)
            cx.s_(f)
            m_sqB[i] = cx.sS.inc(1)
            def f(v, ws=m_sqB[i], wprev=wpv):
                v.wait_ge(cx.sS.h, ws)
                if wprev:
                    v.wait_ge(cx.sV.h, wprev)
                v.tensor_scalar(cx.stat[:, 8:16], cx.stat[:, 0:8], 1.0 / HD, RMS_EPS,
                                ALU.mult, ALU.add).then_inc(cx.sV.h, 1)
            cx.v_(f)
            m_msB = cx.sV.inc(1)
            def f(s, wv=m_msB):
                s.wait_ge(cx.sV.h, wv)
                s.activation(cx.stat[:, 16:24], cx.stat[:, 8:16], AF.Sqrt).then_inc(cx.sS.h, 1)
            cx.s_(f)
            m_sqrtB = cx.sS.inc(1)

            v0 = cx.sV.n
            def f(v, slot=slot, ws=m_sqrtB, v0=v0, hh=hh, wc=m_c,
                  wexp=max(last_exp_masked), wkt=0):
                v.wait_ge(cx.sS.h, ws)
                v.wait_ge(cx.dy.h, wc)
                if wexp:
                    v.wait_ge(cx.sS.h, wexp)
                v.reciprocal(cx.stat[:, 24:32], cx.stat[:, 16:24]).then_inc(cx.sV.h, 1)
                v.wait_ge(cx.sV.h, v0 + 1)
                v.tensor_tensor(cx.stat[:, 40:48], cx.stat[:, 24:32],
                                cx.gainc[:, hh * 8:(hh + 1) * 8], ALU.mult).then_inc(cx.sV.h, 1)
                co = cosst.ap()[:, slot, 0:512].rearrange("q (h n) -> q h n", h=16)
                n = 2
                for pi, pp in enumerate((pQa, pQb)):
                    k1 = pp.rearrange("q (h n) -> q h n", h=4)
                    kqv = kq.ap()[:, pi * 512:(pi + 1) * 512].rearrange("q (h n) -> q h n", h=4)
                    rs = ropes.ap().rearrange("q (h n) -> q h n", h=8)[:, 0:4, :]
                    cob = co[:, pi * 4:pi * 4 + 4, :]
                    csb = co[:, 8 + pi * 4:8 + pi * 4 + 4, :]
                    v.tensor_tensor(kqv[:, :, 0:32], k1[:, :, 0:32], cob, ALU.mult).then_inc(cx.sV.h, 1)
                    v.tensor_tensor(rs, k1[:, :, 32:64], csb, ALU.mult).then_inc(cx.sV.h, 1)
                    n += 2
                    v.wait_ge(cx.sV.h, v0 + n)
                    v.tensor_tensor(kqv[:, :, 0:32], kqv[:, :, 0:32], rs, ALU.add).then_inc(cx.sV.h, 1)
                    v.tensor_tensor(kqv[:, :, 32:64], k1[:, :, 32:64], cob, ALU.mult).then_inc(cx.sV.h, 1)
                    v.tensor_tensor(rs, k1[:, :, 0:32], csb, ALU.mult).then_inc(cx.sV.h, 1)
                    n += 3
                    v.wait_ge(cx.sV.h, v0 + n)
                    v.tensor_tensor(kqv[:, :, 32:64], kqv[:, :, 32:64], rs, ALU.subtract).then_inc(cx.sV.h, 1)
                    v.tensor_copy(kqv[:, :, 64:128], k1[:, :, 64:128]).then_inc(cx.sV.h, 1)
                    n += 2
                v.wait_ge(cx.sV.h, v0 + 16)
                for h8 in range(8):
                    v.tensor_scalar(kb[:, h8 * 128:(h8 + 1) * 128], kq[:, h8 * 128:(h8 + 1) * 128],
                                    cx.stat[:, 40 + h8:41 + h8], None, ALU.mult).then_inc(cx.sV.h, 1)
            cx.v_(f)
            cx.sV.inc(24)
            m_kbq[i] = cx.sV.n
            prev_pQ_free_v = m_kbq[i]
            prev_pQ_free_s = m_sqB[i]

            for b4 in range(2):
                def f(t, b4=b4, wv=m_kbq[i], wev=prev_p6_v):
                    t.wait_ge(cx.sV.h, wv)
                    if wev:
                        t.wait_ge(cx.sV.h, wev)
                    for jj in range(4):
                        h8 = b4 * 4 + jj
                        ins = t.matmul(pT2[:, jj * 128:(jj + 1) * 128],
                                       kb[:, h8 * 128:(h8 + 1) * 128], cx.ident[:],
                                       start=True, stop=True)
                    ins.then_inc(cx.sT.h, 1)
                cx.t_(f)
                mt = cx.sT.inc(1)
                def f(v, b4=b4, i=i, mt=mt):
                    v.wait_ge(cx.sT.h, mt)
                    v.tensor_copy(qT.ap()[:, b4 * 4:(b4 + 1) * 4, i * 128:(i + 1) * 128],
                                  pT2.rearrange("q (h n) -> q h n", h=4)).then_inc(cx.sV.h, 1)
                cx.v_(f)
                m_qTev = cx.sV.inc(1)
                prev_p6_v = m_qTev

        if "qT" in cx.debug and hh == 0 and j == 3:
            def f(y, wv=m_qTev, wk=m_kbq[3]):
                y.wait_ge(cx.sV.h, wv)
                y.dma_start(d["dbg_qT"].rearrange("p h n -> p (h n)"),
                            qT.ap().rearrange("p h n -> p (h n)")).then_inc(cx.dy.h, 16)
                y.dma_start(d["dbg_kb"], kq[:]).then_inc(cx.dy.h, 16)
            cx.y_(f)
            cx.dy.inc(32)
        nch = 8 * (j + 1)
        if not hasattr(cx, "_prev_head_avcs"):
            cx._prev_head_avcs = 0
        for h8 in range(8):
            hg = hh * 8 + h8
            kvl = h8 // 2
            nch_eff0 = nch
            m_exp = [0] * nch
            m_avcs = [0] * nch
            nch_eff = nch - (int(getattr(cx, 'TRUNC', 0)) if j == 3 else 0)
            for kc in range(nch_eff):
                pS = pSS[kc % 2]
                slP = kc % 4
                need_mask = kc >= nch - 8
                if getattr(cx, 'NOMASK30', False) and j == 3 and kc == 30:
                    need_mask = False
                o = kc - (nch - 8)
                def f(t, kc=kc, pS=pS, kvl=kvl, h8=h8, o=o, need_mask=need_mask,
                      wq=m_qTev, wexp=(m_exp[kc - 2] if kc >= 2 else 0),
                      wfv=(prev_headv if kc <= 1 else 0)):
                    if kc == 0:
                        t.wait_ge(cx.sV.h, wq)
                    if wfv:
                        t.wait_ge(cx.sV.h, wfv)
                    if wexp:
                        t.wait_ge(cx.sS.h, wexp)
                    ins = t.matmul(pS[:, 0:512], KT[:, kvl, kc * 128:(kc + 1) * 128],
                                   qT[:, h8, :], start=True, stop=not need_mask)
                    if need_mask:
                        ins = t.matmul(pS[:, 0:512], cx.ident[:], maskq[:, o, :],
                                       start=False, stop=True)
                    ins.then_inc(cx.sT.h, 1)
                cx.t_(f)
                m_sc = cx.sT.inc(1)
                def f(s, kc=kc, pS=pS, slP=slP, hg=hg, mt=m_sc,
                      wav=(m_avcs[kc - 4] if kc >= 4 else cx._prev_head_avcs),
                      wnb=cx.negB_ready):
                    if kc == 0:
                        s.wait_ge(cx.sV.h, wnb)
                    s.wait_ge(cx.sT.h, mt)
                    if wav:
                        s.wait_ge(cx.sT.h, wav)
                    s.activation(PT[:, slP, :], pS[:, 0:512], AF.Exp,
                                 bias=cx.negB[:, hg:hg + 1]).then_inc(cx.sS.h, 1)
                cx.s_(f)
                m_exp[kc] = cx.sS.inc(1)
                def f(t, kc=kc, slP=slP, kvl=kvl, ws=m_exp[kc], nch=nch,
                      wfv=(prev_headv if kc == 0 else 0)):
                    t.wait_ge(cx.sS.h, ws)
                    if kc == 0 and wfv:
                        t.wait_ge(cx.sV.h, wfv)
                    t.matmul(pO[:, 0:512], Vr[:, kc, kvl * 128:(kvl + 1) * 128],
                             PT[:, slP, :], start=(kc == 0), stop=(kc == nch_eff - 1))
                    t.matmul(pR[0:1, 0:512], cx.ones[:, 0:1], PT[:, slP, :],
                             start=(kc == 0), stop=(kc == nch_eff - 1)).then_inc(cx.sT.h, 1)
                cx.t_(f)
                m_avcs[kc] = cx.sT.inc(1)
            if "C30" in cx.debug and hh == 0 and j == 3 and h8 == 7 and nch_eff >= 31:
                v0d = cx.sV.n
                def f(v, msc=m_sc, mex=m_exp[30], v0d=v0d):
                    v.wait_ge(cx.sT.h, msc)
                    v.wait_ge(cx.sS.h, mex)
                    v.tensor_copy(rb[:], pSS[0][:, 0:512]).then_inc(cx.sV.h, 1)
                cx.v_(f)
                wv = cx.sV.inc(1)
                def f(y, wv=wv):
                    y.wait_ge(cx.sV.h, wv)
                    y.dma_start(d["dbg_sc30"], rb[:]).then_inc(cx.dy.h, 16)
                    y.dma_start(d["dbg_pt30"], PT.ap()[:, 2, :]).then_inc(cx.dy.h, 16)
                    y.dma_start(d["dbg_kt30"], KT.ap()[:, 3, 3840:3968]).then_inc(cx.dy.h, 16)
                    y.dma_start(d["dbg_v30"], Vr.ap()[:, 30, :]).then_inc(cx.dy.h, 16)
                    y.dma_start(d["dbg_qt30"].rearrange("p h n -> p (h n)"),
                                qT.ap().rearrange("p h n -> p (h n)")).then_inc(cx.dy.h, 16)
                cx.y_(f)
                wdmp = cx.dy.inc(80)
                def f(v, w=wdmp):
                    v.wait_ge(cx.dy.h, w)
                cx.v_(f)
                def f(t, w=wdmp):
                    t.wait_ge(cx.dy.h, w)
                cx.t_(f)
            cx._prev_head_avcs = m_avcs[nch_eff - 1]
            v0 = cx.sV.n
            def f(v, mt=m_avcs[nch_eff - 1], v0=v0):
                v.wait_ge(cx.sT.h, mt)
                v.tensor_scalar(rowi[0:1, 0:512], pR[0:1, 0:512], 1e-30, None, ALU.add).then_inc(cx.sV.h, 1)
                v.wait_ge(cx.sV.h, v0 + 1)
                v.reciprocal(mskd[0:1, 0, :], rowi[0:1, 0:512]).then_inc(cx.sV.h, 1)
                v.wait_ge(cx.sV.h, v0 + 2)
                v.tensor_copy(rowb[0:1, :], mskd[0:1, 0, :]).then_inc(cx.sV.h, 1)
            cx.v_(f)
            m_recip = cx.sV.inc(3)
            def f(t, wv=m_recip):
                t.wait_ge(cx.sV.h, wv)
                t.matmul(pR[:, 0:512], cx.ones[0:1, 0:128], rowb[0:1, :],
                         start=True, stop=True).then_inc(cx.sT.h, 1)
            cx.t_(f)
            m_bc = cx.sT.inc(1)
            v0 = cx.sV.n
            def f(v, h8=h8, mt=m_bc, v0=v0, wdma=prev_yT_dma):
                v.wait_ge(cx.sT.h, mt)
                if wdma:
                    v.wait_ge(cx.dy.h, wdma)
                v.tensor_copy(rb[:], pR[:, 0:512]).then_inc(cx.sV.h, 1)
                v.wait_ge(cx.sV.h, v0 + 1)
                v.tensor_tensor(yTg[:, h8, :], pO[:, 0:512], rb[:], ALU.mult).then_inc(cx.sV.h, 1)
            cx.v_(f)
            prev_headv = cx.sV.inc(2)
        def f(y, hh=hh, s0=s0, wv=prev_headv):
            y.wait_ge(cx.sV.h, wv)
            for i in range(4):
                y.dma_start(d["yt_s"][s0 + i, hh * 8:(hh + 1) * 8].rearrange("h p n -> p h n"),
                            yTg.ap()[:, :, i * 128:(i + 1) * 128]).then_inc(cx.dy.h, 16)
        cx.y_(f)
        prev_yT_dma = cx.dy.inc(64)
        last_exp_masked[0] = max(last_exp_masked[0], 0)

    if "KT2" in cx.debug and hh == 0:
        def f(y, wv=cx.sV.n, wt=cx.sT.n):
            y.wait_ge(cx.sV.h, wv)
            y.wait_ge(cx.sT.h, wt)
            y.dma_start(d["dbg_KT"][1], KT[:]).then_inc(cx.dy.h, 16)
            y.dma_start(d["dbg_V"][1], Vr[:]).then_inc(cx.dy.h, 16)
        cx.y_(f)
        wdmp = cx.dy.inc(32)
        def f(v, w=wdmp):
            v.wait_ge(cx.dy.h, w)
        cx.v_(f)
        def f(t, w=wdmp):
            t.wait_ge(cx.dy.h, w)
        cx.t_(f)
    st.pe_weights_free = cx.sT.n
    st.b_done_sV = cx.sV.n
    st.b_done_sS = cx.sS.n
    st.a_xpose_last = 0
    st.b_xpose_last = max(m_xpose.values()) if m_xpose else 0
    st.b_kb_last = prev_pQ_free_v
    st.b_yT_dma = prev_yT_dma
    cx.b_yT_dma = prev_yT_dma


def _phase_barrier(cx, marks):
    sT, sV, sS = marks["sT"], marks["sV"], marks["sS"]
    dy, dg = cx.dy.n, cx.dg.n
    for emit in (cx.y_, cx.g_):
        def f(e, dy=dy, dg=dg):
            if dy: e.wait_ge(cx.dy.h, dy)
            if dg: e.wait_ge(cx.dg.h, dg)
        emit(f)
    for emit in (cx.t_, cx.v_, cx.s_, cx.g_, cx.y_):
        def f(e, sT=sT, sV=sV, sS=sS):
            if sT: e.wait_ge(cx.sT.h, sT)
            if sV: e.wait_ge(cx.sV.h, sV)
            if sS: e.wait_ge(cx.sS.h, sS)
        emit(f)


def emit_C(cx, es):
    nc = cx.nc
    d = cx.dram
    p = cx.p
    _phase_barrier(cx, cx.AB_done)
    with ExitStack() as ces:
        E = ces.enter_context
        WoT = E(nc.sbuf_tensor("WoT", [128, DC, D], BF16))
        Ao = E(nc.sbuf_tensor("Ao", [128, DC, R], BF16))
        Bo = E(nc.sbuf_tensor("Bo", [R, D], BF16))
        laevC = E(nc.sbuf_tensor("laevC", [R, 128], BF16))
        gwf = E(nc.sbuf_tensor("gwf", [128, 128], F32))
        gwt = E(nc.sbuf_tensor("gwt", [128, 128], F32))
        gwcat = E(nc.sbuf_tensor("gwcat", [128, DC, 16], BF16))
        yTt = E(nc.sbuf_tensor("yTt", [128, 2, DC, 128], BF16))
        xf32 = E(nc.sbuf_tensor("xf32", [128, 2, D], F32))
        x1pre = E(nc.sbuf_tensor("x1pre", [128, D], F32))
        x1f = E(nc.sbuf_tensor("x1f", [128, D], F32))
        x1h = E(nc.sbuf_tensor("x1h", [128, D], BF16))
        x1l = E(nc.sbuf_tensor("x1l", [128, D], BF16))
        sqC = E(nc.sbuf_tensor("sqC", [128, D], F32))
        xstage = E(nc.sbuf_tensor("xstage", [128, 2, DC, 128], BF16))
        xlT = E(nc.sbuf_tensor("xlT", [128, DC, 128], BF16))
        pC = [p[0], p[1], p[2], p[3]]
        pL2, pX2, pG = p[4], p[5], p[6]

        for dst, src in ((WoT, d["wo_sb"]), (Ao, d["ao_sb"]), (Bo, d["bo_sb"])):
            def f(g, dst=dst, src=src):
                g.dma_start(dst[:], src).then_inc(cx.dg.h, 16)
            cx.g_(f); cx.dg.inc(16)
        def f(y):
            y.dma_start(gwf[:], d["gw_sb"].rearrange("p c e -> p (c e)")).then_inc(cx.dy.h, 16)
        cx.y_(f)
        wgw = cx.dy.inc(16)
        wwo = cx.dg.n
        v0 = cx.sV.n
        def f(v, wld=wwo, wgw=wgw, v0=v0):
            v.wait_ge(cx.dg.h, wld)
            v.tensor_scalar(WoT.ap().rearrange("p c n -> p (c n)"),
                            WoT.ap().rearrange("p c n -> p (c n)"),
                            cx.smalls[:, 3:4], None, ALU.mult).then_inc(cx.sV.h, 1)
            v.wait_ge(cx.dy.h, wgw)
            v.tensor_copy(gwcat.ap()[:, :, 0:8], gwf.ap().rearrange("p (c e) -> p c e", e=8)).then_inc(cx.sV.h, 1)
            v.wait_ge(cx.sV.h, v0 + 2)
            v.tensor_copy(gwt.ap().rearrange("p (c e) -> p c e", e=8),
                          gwcat.ap()[:, :, 0:8]).then_inc(cx.sV.h, 1)
            v.wait_ge(cx.sV.h, v0 + 3)
            v.tensor_tensor(gwcat.ap()[:, :, 8:16],
                            gwf.ap().rearrange("p (c e) -> p c e", e=8),
                            gwt.ap().rearrange("p (c e) -> p c e", e=8), ALU.subtract).then_inc(cx.sV.h, 1)
        cx.v_(f)
        w_pre = cx.sV.inc(4)

        cx_lastev = [0]
        m_x1tdma = [0] * NOT
        m_logit = [0] * NOT
        m_oproj = [0] * NOT
        m_x1pre = [0] * NOT
        m_gate = [0] * NOT
        m_gsel = [0] * NOT
        m_sq2 = [0] * NOT
        m_hT = [0] * NOT
        m_lT = [0] * NOT
        m_x1dma = [0] * NOT
        m_loads_y = [0] * NOT
        m_loads_x = [0] * NOT

        def issue_loads(t):
            slot = t % 2
            wT = m_oproj[t - 2] if t >= 2 else 0
            def f(y, t=t, slot=slot, wT=wT):
                if wT:
                    y.wait_ge(cx.sT.h, wT)
                y.dma_start(yTt[:, slot, :, :],
                            d["yt_s"][t].rearrange("c p n -> p c n")).then_inc(cx.dy.h, 16)
            cx.y_(f)
            m_loads_y[t] = cx.dy.inc(16)
            wV = m_x1pre[t - 2] if t >= 2 else 0
            def f(y, t=t, slot=slot, wV=wV):
                if wV:
                    y.wait_ge(cx.sV.h, wV)
                y.dma_start(xf32[:, slot, :], d["xq"][t * 128:(t + 1) * 128, :]).then_inc(cx.dy.h, 16)
            cx.y_(f)
            m_loads_x[t] = cx.dy.inc(16)

        issue_loads(0)
        for t in range(NOT):
            slot = t % 2
            if t + 1 < NOT:
                issue_loads(t + 1)
            def f(tt, t=t, slot=slot, wy=m_loads_y[t], wv=w_pre, wldg=wwo,
                  wfree=(m_x1pre[t - 1] if t else 0)):
                tt.wait_ge(cx.dg.h, wldg)
                tt.wait_ge(cx.dy.h, wy)
                tt.wait_ge(cx.sV.h, wv)
                if wfree:
                    tt.wait_ge(cx.sV.h, wfree)
                for fc in range(DC):
                    for jj in range(4):
                        tt.matmul(pC[jj][:, 0:512], yTt[:, slot, fc, :],
                                  WoT[:, fc, jj * 512:(jj + 1) * 512],
                                  start=(fc == 0), stop=False)
                for fc in range(DC):
                    ins = tt.matmul(pL2[0:32, 0:128], Ao[:, fc, :], yTt[:, slot, fc, :],
                                    start=(fc == 0), stop=(fc == DC - 1))
                ins.then_inc(cx.sT.h, 1)
            cx.t_(f)
            m_l = cx.sT.inc(1)
            def f(v, mt=m_l):
                v.wait_ge(cx.sT.h, mt)
                v.tensor_copy(laevC[:, :], pL2[0:32, 0:128]).then_inc(cx.sV.h, 1)
            cx.v_(f)
            m_le = cx.sV.inc(1)
            def f(tt, wv=m_le):
                tt.wait_ge(cx.sV.h, wv)
                for jj in range(4):
                    ins = tt.matmul(pC[jj][:, 0:512], laevC[:, :], Bo[:, jj * 512:(jj + 1) * 512],
                                    start=False, stop=True)
                ins.then_inc(cx.sT.h, 1)
            cx.t_(f)
            m_oproj[t] = cx.sT.inc(1)

            def f(v, t=t, slot=slot, mt=m_oproj[t], wx=m_loads_x[t],
                  wst=(m_gsel[t - 1] if t else 0)):
                v.wait_ge(cx.sT.h, mt)
                v.wait_ge(cx.dy.h, wx)
                if wst:
                    v.wait_ge(cx.sV.h, wst)
                v.scalar_tensor_tensor(x1pre[:], cx.pAll.ap()[:, 0:2048], 1.0, xf32[:, slot, :],
                                       ALU.mult, ALU.add, accum_out=cx.stat[:, 48:49]).then_inc(cx.sV.h, 1)
            cx.v_(f)
            m_x1pre[t] = cx.sV.inc(1)
            def f(s, wv=m_x1pre[t], wfree=(m_sq2[t - 1] if t else 0)):
                s.wait_ge(cx.sV.h, wv)
                s.activation(sqC[:], x1pre[:], AF.Square, accum_out=cx.stat[:, 49:50]).then_inc(cx.sS.h, 1)
            cx.s_(f)
            m_sq2[t] = cx.sS.inc(1)
            v0 = cx.sV.n
            def f(v, ws=m_sq2[t], v0=v0):
                v.wait_ge(cx.sS.h, ws)
                v.tensor_scalar(cx.stat[:, 50:52], cx.stat[:, 48:50], 1.0 / D, None, ALU.mult).then_inc(cx.sV.h, 1)
                v.wait_ge(cx.sV.h, v0 + 1)
                v.tensor_tensor(cx.stat[:, 52:53], cx.stat[:, 50:51], cx.stat[:, 50:51], ALU.mult).then_inc(cx.sV.h, 1)
                v.wait_ge(cx.sV.h, v0 + 2)
                v.tensor_tensor(cx.stat[:, 53:54], cx.stat[:, 51:52], cx.stat[:, 52:53], ALU.subtract).then_inc(cx.sV.h, 1)
            cx.v_(f)
            m_var = cx.sV.inc(3)
            def f(s, wv=m_var):
                s.wait_ge(cx.sV.h, wv)
                s.activation(cx.stat[:, 54:55], cx.stat[:, 53:54], AF.Sqrt, bias=cx.smalls[:, 8:9]).then_inc(cx.sS.h, 1)
            cx.s_(f)
            m_std = cx.sS.inc(1)
            v0 = cx.sV.n
            def f(v, ws=m_std, v0=v0, t=t, wd=(m_x1dma[t - 1] if t else 0)):
                v.wait_ge(cx.sS.h, ws)
                if wd:
                    v.wait_ge(cx.dy.h, wd)
                v.reciprocal(cx.stat[:, 55:56], cx.stat[:, 54:55]).then_inc(cx.sV.h, 1)
                v.wait_ge(cx.sV.h, v0 + 1)
                v.tensor_scalar(x1f[:], x1pre[:], cx.stat[:, 50:51], cx.stat[:, 55:56],
                                ALU.subtract, ALU.mult).then_inc(cx.sV.h, 1)
                v.wait_ge(cx.sV.h, v0 + 2)
                v.tensor_copy(x1h[:], x1f[:]).then_inc(cx.sV.h, 1)
                v.wait_ge(cx.sV.h, v0 + 3)
                v.tensor_copy(sqC[:], x1h[:]).then_inc(cx.sV.h, 1)
                v.wait_ge(cx.sV.h, v0 + 4)
                v.tensor_tensor(x1l[:], x1f[:], sqC[:], ALU.subtract).then_inc(cx.sV.h, 1)
            cx.v_(f)
            m_x1 = cx.sV.inc(5)
            def f(y, t=t, wv=m_x1):
                y.wait_ge(cx.sV.h, wv)
                y.dma_start(d["x1_s"][t], x1f[:]).then_inc(cx.dy.h, 16)
            cx.y_(f)
            m_x1dma[t] = cx.dy.inc(16)

            for which, wsrc in (("h", m_x1), ("l", m_x1)):
                for b4 in range(4):
                    def f(tt, which=which, b4=b4, slot=slot, wv=wsrc,
                          wev=cx_lastev[0]):
                        if b4 == 0:
                            tt.wait_ge(cx.sV.h, wv)
                        if wev:
                            tt.wait_ge(cx.sV.h, wev)
                        srcb = x1h if which == "h" else x1l
                        for jj in range(4):
                            c = b4 * 4 + jj
                            ins = tt.matmul(pX2[:, jj * 128:(jj + 1) * 128],
                                            srcb[:, c * 128:(c + 1) * 128], cx.ident[:],
                                            start=True, stop=True)
                        ins.then_inc(cx.sT.h, 1)
                    cx.t_(f)
                    mt = cx.sT.inc(1)
                    def f(v, which=which, b4=b4, slot=slot, mt=mt,
                          wd=(m_x1tdma[t - 2] if (which == "h" and t >= 2) else 0)):
                        v.wait_ge(cx.sT.h, mt)
                        if wd:
                            v.wait_ge(cx.dy.h, wd)
                        dst = xstage.ap()[:, slot, b4 * 4:(b4 + 1) * 4, :] if which == "h" \
                            else xlT.ap()[:, b4 * 4:(b4 + 1) * 4, :]
                        v.tensor_copy(dst, pX2.rearrange("q (c n) -> q c n", c=4)).then_inc(cx.sV.h, 1)
                    cx.v_(f)
                    cx_lastev[0] = cx.sV.inc(1)
                if which == "h":
                    m_hT[t] = cx_lastev[0]
                else:
                    m_lT[t] = cx_lastev[0]
            def f(y, t=t, slot=slot, wv=m_hT[t]):
                y.wait_ge(cx.sV.h, wv)
                y.dma_start(d["x1t_s"][:, :, t * 128:(t + 1) * 128],
                            xstage[:, slot, :, :]).then_inc(cx.dy.h, 16)
            cx.y_(f)
            m_x1tdma[t] = cx.dy.inc(16)

            def f(tt, t=t, slot=slot, wh=m_hT[t], wl=m_lT[t],
                  wfree=(m_logit[t - 1] if t else 0)):
                tt.wait_ge(cx.sV.h, wl)
                if wfree:
                    tt.wait_ge(cx.sV.h, wfree)
                for dc in range(DC):
                    tt.matmul(pG[:, 0:16], xstage[:, slot, dc, :], gwcat[:, dc, :],
                              start=(dc == 0), stop=False)
                for dc in range(DC):
                    ins = tt.matmul(pG[:, 0:8], xlT[:, dc, :], gwcat[:, dc, 0:8],
                                    start=False, stop=(dc == DC - 1))
                ins.then_inc(cx.sT.h, 1)
            cx.t_(f)
            m_gate[t] = cx.sT.inc(1)
            v0 = cx.sV.n
            def f(v, t=t, mt=m_gate[t], v0=v0):
                v.wait_ge(cx.sT.h, mt)
                v.tensor_copy(cx.stat[:, 0:16], pG[:, 0:16]).then_inc(cx.sV.h, 1)
                v.wait_ge(cx.sV.h, v0 + 1)
                v.tensor_tensor(cx.stat[:, 16:24], cx.stat[:, 0:8], cx.stat[:, 8:16], ALU.add).then_inc(cx.sV.h, 1)
            cx.v_(f)
            m_logit[t] = cx.sV.inc(2)
            def f(s, wv=m_logit[t]):
                s.wait_ge(cx.sV.h, wv)
                s.activation(cx.stat[:, 24:32], cx.stat[:, 16:24], AF.Sigmoid).then_inc(cx.sS.h, 1)
            cx.s_(f)
            m_sig = cx.sS.inc(1)
            v0 = cx.sV.n
            def f(v, t=t, ws=m_sig, v0=v0):
                v.wait_ge(cx.sS.h, ws)
                v.tensor_tensor(cx.stat[:, 32:40], cx.stat[:, 24:32], cx.smalls[:, 32:40], ALU.add).then_inc(cx.sV.h, 1)
                v.wait_ge(cx.sV.h, v0 + 1)
                v.reduce_max(cx.stat[:, 40:41], cx.stat[:, 32:40], axis=AX).then_inc(cx.sV.h, 1)
                v.wait_ge(cx.sV.h, v0 + 2)
                v.tensor_scalar(cx.stat[:, 41:49] if False else cx.stat[:, 56:64],
                                cx.stat[:, 32:40], cx.stat[:, 40:41], None, ALU.is_equal).then_inc(cx.sV.h, 1)
                v.wait_ge(cx.sV.h, v0 + 3)
                v.scalar_tensor_tensor(cx.stat[:, 56:64], cx.stat[:, 56:64], -1e30,
                                       cx.stat[:, 32:40], ALU.mult, ALU.add).then_inc(cx.sV.h, 1)
                v.wait_ge(cx.sV.h, v0 + 4)
                v.reduce_max(cx.stat[:, 41:42], cx.stat[:, 56:64], axis=AX).then_inc(cx.sV.h, 1)
                v.wait_ge(cx.sV.h, v0 + 5)
                v.tensor_scalar(cx.stat[:, 56:64], cx.stat[:, 32:40], cx.stat[:, 41:42], None, ALU.is_ge).then_inc(cx.sV.h, 1)
                v.wait_ge(cx.sV.h, v0 + 6)
                v.tensor_tensor(cx.gsel[:, t, :], cx.stat[:, 24:32], cx.stat[:, 56:64], ALU.mult).then_inc(cx.sV.h, 1)
                v.wait_ge(cx.sV.h, v0 + 7)
                v.tensor_copy(cx.gselb[:, t, :], cx.gsel[:, t, :]).then_inc(cx.sV.h, 1)
            cx.v_(f)
            m_gsel[t] = cx.sV.inc(8)

    cx.C_done = dict(sT=cx.sT.n, sV=cx.sV.n, sS=cx.sS.n)
    if "x1" in cx.debug:
        def f(y, wv=cx.sV.n, wdy=cx.dy.n):
            y.wait_ge(cx.sV.h, wv)
            y.wait_ge(cx.dy.h, wdy)
            y.dma_start(d["dbg_x1"].rearrange("t p n -> (t p) n"),
                        d["x1_s"].rearrange("t p n -> (t p) n")).then_inc(cx.dy.h, 16)
            y.dma_start(d["dbg_x1t"].rearrange("p c n -> (p c) n"),
                        d["x1t_s"].rearrange("p c n -> (p c) n")).then_inc(cx.dy.h, 16)
        cx.y_(f)
        cx.dy.inc(32)
        def f(y, wv=cx.sV.n):
            y.wait_ge(cx.sV.h, wv)
            y.dma_start(d["dbg_gsel"].rearrange("p t e -> p (t e)"),
                        cx.gsel.ap().rearrange("p t e -> p (t e)")).then_inc(cx.dy.h, 16)
        cx.y_(f)
        cx.dy.inc(16)


def emit_DE(cx, es, do_E):
    nc = cx.nc
    d = cx.dram
    p = cx.p
    _phase_barrier(cx, cx.C_done)
    with ExitStack() as des:
        E = des.enter_context
        x1Th = E(nc.sbuf_tensor("x1Th", [128, DC, 1024], BF16))
        ACC = E(nc.sbuf_tensor("ACC", [128, 8, D], F32))
        hT = E(nc.sbuf_tensor("hT", [128, HC, 1024], BF16))
        w1r = E(nc.sbuf_tensor("w1r", [128, 2, DC, 128], BF16))
        w3r = E(nc.sbuf_tensor("w3r", [128, 2, DC, 128], BF16))
        w2all = E(nc.sbuf_tensor("w2all", [128, HC, D], BF16))
        t1 = E(nc.sbuf_tensor("t1", [128, 1024], F32))
        t2 = E(nc.sbuf_tensor("t2", [128, 1024], F32))
        gb = E(nc.sbuf_tensor("gbt", [128, 1024], BF16))
        gTs = E(nc.sbuf_tensor("gTs", [EXP, NOT * 128], BF16))
        xE = E(nc.sbuf_tensor("xE", [128, D], F32))

        def f(tt, wv=cx.C_done["sV"]):
            for t in range(NOT):
                ins = tt.matmul(cx.pAll.ap()[0:8, t * 128:(t + 1) * 128],
                                cx.gselb[:, t, :], cx.ident[:], start=True, stop=True)
            ins.then_inc(cx.sT.h, 1)
        cx.t_(f)
        m_gt = cx.sT.inc(1)
        def f(v, mt=m_gt):
            v.wait_ge(cx.sT.h, mt)
            v.tensor_copy(gTs[:, :], cx.pAll.ap()[0:8, 0:2048]).then_inc(cx.sV.h, 1)
        cx.v_(f)
        m_gTs = cx.sV.inc(1)

        m_w13 = {}
        m_w2ld = {}
        m_uv = {}
        m_h = {}
        m_w2 = {}
        m_acc = {}
        prev_w2_done = 0
        prev_uv_read = 0
        last_acc_v = 0
        m_out_dma = [0] * NOT

        for hf in range(2):
            if hf:
                _phase_barrier(cx, dict(sT=cx.sT.n, sV=cx.sV.n, sS=cx.sS.n))
            def f(y, hf=hf, wfree=prev_uv_read):
                if wfree:
                    y.wait_ge(cx.sT.h, wfree)
                for half2 in range(2):
                    y.dma_start(x1Th[:, half2 * 8:(half2 + 1) * 8, :],
                                d["x1t_s"][:, half2 * 8:(half2 + 1) * 8, hf * 1024:(hf + 1) * 1024]).then_inc(cx.dy.h, 16)
            cx.y_(f)
            m_x1Th = cx.dy.inc(32)

            for e in range(getattr(cx, 'DEXP', EXP + 1)):
                def f(g, e=e, hf=hf, wfree=prev_uv_read, w2free=prev_w2_done):
                    if wfree:
                        g.wait_ge(cx.sT.h, wfree)
                    g.dma_start(w1r[:, 0, :, :], d["w1_sb"][e, 0]).then_inc(cx.dg.h, 16)
                    g.dma_start(w3r[:, 0, :, :], d["w3_sb"][e, 0]).then_inc(cx.dg.h, 16)
                    if w2free:
                        g.wait_ge(cx.sT.h, w2free)
                    for hc in range(HC):
                        g.dma_start(w2all[:, hc, :], d["w2_sb"][e, hc]).then_inc(cx.dg.h, 16)
                cx.g_(f)
                cx.dg.inc(32)
                m_w13[(e, 0)] = cx.dg.n - 16 * HC
                m_w2ld[e] = cx.dg.inc(16 * HC)

                if e < EXP:
                    def f(tt, e=e, hf=hf, wg=m_gTs, wfree=last_acc_v):
                        tt.wait_ge(cx.sV.h, wg)
                        if wfree:
                            tt.wait_ge(cx.sV.h, wfree)
                        for jj in range(2):
                            ins = tt.matmul(cx.pAll.ap()[:, (4 + jj) * 512:(5 + jj) * 512],
                                            cx.sel8[0:8, e * 128:(e + 1) * 128],
                                            gTs[0:8, hf * 1024 + jj * 512:hf * 1024 + (jj + 1) * 512],
                                            start=True, stop=True)
                        ins.then_inc(cx.sT.h, 1)
                    cx.t_(f)
                    m_gb = cx.sT.inc(1)
                    def f(v, mt=m_gb):
                        v.wait_ge(cx.sT.h, mt)
                        v.tensor_copy(gb[:], cx.pAll.ap()[:, 2048:3072]).then_inc(cx.sV.h, 1)
                    cx.v_(f)
                    m_gbe = cx.sV.inc(1)
                else:
                    m_gbe = 0

                for hc in range(HC):
                    slot = hc % 2
                    if hc + 1 < HC:
                        def f(g, e=e, hc=hc, wfree=(m_uv.get((e, hc - 1), 0))):
                            if wfree:
                                g.wait_ge(cx.sT.h, wfree)
                            g.dma_start(w1r[:, (hc + 1) % 2, :, :], d["w1_sb"][e, hc + 1]).then_inc(cx.dg.h, 16)
                            g.dma_start(w3r[:, (hc + 1) % 2, :, :], d["w3_sb"][e, hc + 1]).then_inc(cx.dg.h, 16)
                        cx.g_(f)
                        m_w13[(e, hc + 1)] = cx.dg.inc(32) - 16
                    def f(tt, e=e, hc=hc, slot=slot, wld=m_w13[(e, hc)] + 16,
                          wx=m_x1Th, whf=(m_h.get((e, hc - 1), 0)),
                          wacc=max(last_acc_v, m_gTs)):
                        tt.wait_ge(cx.dg.h, wld)
                        if hc == 0:
                            tt.wait_ge(cx.dy.h, wx)
                            if wacc:
                                tt.wait_ge(cx.sV.h, wacc)
                        if whf:
                            tt.wait_ge(cx.sV.h, whf)
                        for dc in range(DC):
                            tt.matmul(cx.pAll.ap()[:, 0:512], w1r[:, slot, dc, :],
                                      x1Th[:, dc, 0:512], start=(dc == 0), stop=False)
                            tt.matmul(cx.pAll.ap()[:, 512:1024], w1r[:, slot, dc, :],
                                      x1Th[:, dc, 512:1024], start=(dc == 0), stop=False)
                            tt.matmul(cx.pAll.ap()[:, 1024:1536], w3r[:, slot, dc, :],
                                      x1Th[:, dc, 0:512], start=(dc == 0), stop=False)
                            ins = tt.matmul(cx.pAll.ap()[:, 1536:2048], w3r[:, slot, dc, :],
                                            x1Th[:, dc, 512:1024], start=(dc == 0), stop=(dc == DC - 1))
                        ins.then_inc(cx.sT.h, 1)
                    cx.t_(f)
                    m_uv[(e, hc)] = cx.sT.inc(1)
                    prev_uv_read = m_uv[(e, hc)]

                    if e < EXP:
                        def f(v, mt=m_uv[(e, hc)], wg=m_gbe):
                            v.wait_ge(cx.sT.h, mt)
                            v.wait_ge(cx.sV.h, wg)
                            v.tensor_tensor(t1[:], cx.pAll.ap()[:, 0:1024], gb[:], ALU.mult).then_inc(cx.sV.h, 1)
                        cx.v_(f)
                        m_t1 = cx.sV.inc(1)
                        def f(s, wv=m_t1):
                            s.wait_ge(cx.sV.h, wv)
                            s.activation(t1[:], t1[:], AF.Silu).then_inc(cx.sS.h, 1)
                        cx.s_(f)
                        m_sil = cx.sS.inc(1)
                        v0 = cx.sV.n
                        def f(v, e=e, hc=hc, ws=m_sil, v0=v0, wg=m_gbe):
                            v.tensor_tensor(t2[:], cx.pAll.ap()[:, 1024:2048], gb[:], ALU.mult).then_inc(cx.sV.h, 1)
                            v.wait_ge(cx.sS.h, ws)
                            v.wait_ge(cx.sV.h, v0 + 1)
                            v.tensor_tensor(hT[:, hc, :], t1[:], t2[:], ALU.mult).then_inc(cx.sV.h, 1)
                        cx.v_(f)
                        m_h[(e, hc)] = cx.sV.inc(2)
                    else:
                        def f(s, mt=m_uv[(e, hc)]):
                            s.wait_ge(cx.sT.h, mt)
                            s.activation(t1[:], cx.pAll.ap()[:, 0:1024], AF.Silu).then_inc(cx.sS.h, 1)
                        cx.s_(f)
                        m_sil = cx.sS.inc(1)
                        v0 = cx.sV.n
                        def f(v, hc=hc, ws=m_sil, v0=v0, mt=m_uv[(e, hc)]):
                            v.wait_ge(cx.sT.h, mt)
                            v.tensor_copy(t2[:], cx.pAll.ap()[:, 1024:2048]).then_inc(cx.sV.h, 1)
                            v.wait_ge(cx.sS.h, ws)
                            v.wait_ge(cx.sV.h, v0 + 1)
                            v.tensor_tensor(hT[:, hc, :], t1[:], t2[:], ALU.mult).then_inc(cx.sV.h, 1)
                        cx.v_(f)
                        m_h[(e, hc)] = cx.sV.inc(2)

                for tp in range(4):
                    def f(tt, e=e, tp=tp, wld=m_w2ld[e], wh=m_h[(e, HC - 1)],
                          wacc=last_acc_v):
                        tt.wait_ge(cx.dg.h, wld)
                        tt.wait_ge(cx.sV.h, wh)
                        if wacc:
                            tt.wait_ge(cx.sV.h, wacc)
                        for hc in range(HC):
                            for ti in range(2):
                                tc = 2 * tp + ti
                                for jj in range(4):
                                    ins = tt.matmul(
                                        cx.pAll.ap()[:, (ti * 4 + jj) * 512:(ti * 4 + jj + 1) * 512],
                                        hT[:, hc, tc * 128:(tc + 1) * 128],
                                        w2all[:, hc, jj * 512:(jj + 1) * 512],
                                        start=(hc == 0), stop=(hc == HC - 1))
                        ins.then_inc(cx.sT.h, 1)
                    cx.t_(f)
                    m_w2[(e, tp)] = cx.sT.inc(1)
                    def f(v, e=e, hf=hf, tp=tp, mt=m_w2[(e, tp)]):
                        v.wait_ge(cx.sT.h, mt)
                        for ti in range(2):
                            tc = 2 * tp + ti
                            if e == 0:
                                v.tensor_copy(ACC[:, tc, :], cx.pAll.ap()[:, ti * 2048:(ti + 1) * 2048]).then_inc(cx.sV.h, 1)
                            else:
                                v.tensor_tensor(ACC[:, tc, :], ACC[:, tc, :],
                                                cx.pAll.ap()[:, ti * 2048:(ti + 1) * 2048], ALU.add).then_inc(cx.sV.h, 1)
                    cx.v_(f)
                    m_acc[(e, tp)] = cx.sV.inc(2)
                    last_acc_v = m_acc[(e, tp)]
                prev_w2_done = m_w2[(e, 3)]

            if "ACC" in cx.debug and hf == 0:
                def f(y, wv=last_acc_v):
                    y.wait_ge(cx.sV.h, wv)
                    y.dma_start(d["dbg_ACC"].rearrange("p c n -> p (c n)"),
                                ACC.ap().rearrange("p c n -> p (c n)")).then_inc(cx.dy.h, 16)
                cx.y_(f)
                wdmp = cx.dy.inc(16)
                def f(v, w=wdmp):
                    v.wait_ge(cx.dy.h, w)
                cx.v_(f)
            _phase_barrier(cx, dict(sT=cx.sT.n, sV=cx.sV.n, sS=cx.sS.n))
            if do_E:
                for tc in range(8):
                    t = hf * 8 + tc
                    def f(y, t=t, wd=(m_out_dma[t - 1] if t else 0)):
                        if wd:
                            y.wait_ge(cx.dy.h, wd)
                        y.dma_start(xE[:], d["x1_s"][t]).then_inc(cx.dy.h, 16)
                    cx.y_(f)
                    m_xe = cx.dy.inc(16)
                    v0 = cx.sV.n
                    def f(v, tc=tc, wy=m_xe, v0=v0, wacc=last_acc_v):
                        v.wait_ge(cx.dy.h, wy)
                        v.wait_ge(cx.sV.h, wacc)
                        v.scalar_tensor_tensor(ACC[:, tc, :], ACC[:, tc, :], 1.0, xE[:],
                                               ALU.mult, ALU.add, accum_out=cx.stat[:, 48:49]).then_inc(cx.sV.h, 1)
                    cx.v_(f)
                    m_pre = cx.sV.inc(1)
                    def f(s, tc=tc, wv=m_pre):
                        s.wait_ge(cx.sV.h, wv)
                        s.activation(hT.ap().rearrange("p c n -> p (c n)")[:, 0:2048],
                                     ACC[:, tc, :], AF.Square, accum_out=cx.stat[:, 49:50]).then_inc(cx.sS.h, 1)
                    cx.s_(f)
                    m_sqE = cx.sS.inc(1)
                    v0 = cx.sV.n
                    def f(v, ws=m_sqE, v0=v0):
                        v.wait_ge(cx.sS.h, ws)
                        v.tensor_scalar(cx.stat[:, 50:52], cx.stat[:, 48:50], 1.0 / D, None, ALU.mult).then_inc(cx.sV.h, 1)
                        v.wait_ge(cx.sV.h, v0 + 1)
                        v.tensor_tensor(cx.stat[:, 52:53], cx.stat[:, 50:51], cx.stat[:, 50:51], ALU.mult).then_inc(cx.sV.h, 1)
                        v.wait_ge(cx.sV.h, v0 + 2)
                        v.tensor_tensor(cx.stat[:, 53:54], cx.stat[:, 51:52], cx.stat[:, 52:53], ALU.subtract).then_inc(cx.sV.h, 1)
                    cx.v_(f)
                    m_varE = cx.sV.inc(3)
                    def f(s, wv=m_varE):
                        s.wait_ge(cx.sV.h, wv)
                        s.activation(cx.stat[:, 54:55], cx.stat[:, 53:54], AF.Sqrt, bias=cx.smalls[:, 8:9]).then_inc(cx.sS.h, 1)
                    cx.s_(f)
                    m_stdE = cx.sS.inc(1)
                    v0 = cx.sV.n
                    def f(v, tc=tc, ws=m_stdE, v0=v0):
                        v.wait_ge(cx.sS.h, ws)
                        v.reciprocal(cx.stat[:, 55:56], cx.stat[:, 54:55]).then_inc(cx.sV.h, 1)
                        v.wait_ge(cx.sV.h, v0 + 1)
                        v.tensor_scalar(xE[:], ACC[:, tc, :], cx.stat[:, 50:51], cx.stat[:, 55:56],
                                        ALU.subtract, ALU.mult).then_inc(cx.sV.h, 1)
                    cx.v_(f)
                    m_outv = cx.sV.inc(2)
                    def f(y, t=t, wv=m_outv):
                        y.wait_ge(cx.sV.h, wv)
                        y.dma_start(d["out"][t * 128:(t + 1) * 128, :], xE[:]).then_inc(cx.dy.h, 16)
                    cx.y_(f)
                    m_out_dma[t] = cx.dy.inc(16)
                    last_acc_v = m_outv


def emit_tail(cx):
    d = cx.dram
    if "yT" in cx.debug:
        def f(y, wd=getattr(cx, "b_yT_dma", 0), wv=cx.sV.n):
            y.wait_ge(cx.dy.h, wd)
            y.wait_ge(cx.sV.h, wv)
            y.dma_start(d["dbg_yT"].rearrange("t c p n -> (t c p) n"),
                        d["yt_s"].rearrange("t c p n -> (t c p) n")).then_inc(cx.dy.h, 16)
        cx.y_(f)
        cx.dy.inc(16)
    def f(y, wy=cx.dy.n, wg=cx.dg.n):
        y.wait_ge(cx.dy.h, wy)
        y.wait_ge(cx.dg.h, wg)
    cx.y_(f)


def kernel(**inputs):
    per_core = host_prep(inputs)
    nc = build_program()
    res = run_bass_kernel_spmd(nc, per_core, core_ids=list(range(8)))
    out = np.empty((B, S, D), np.float32)
    for c in range(8):
        b, hh = c // 2, c % 2
        o = res.results[c]["out"].reshape(NOT, 128, D)
        for s in range(NOT):
            out[b, (2 * s + hh) * 128:(2 * s + hh + 1) * 128] = o[s]
    return out
